# version 07 (final)
# baseline (speedup 1.0000x reference)
import numpy as np
import ml_dtypes
from contextlib import ExitStack

import concourse.bass as bass
import concourse.mybir as mybir
from concourse import bass_isa
from concourse.bass_utils import run_bass_kernel_spmd

AF = mybir.ActivationFunctionType
OP = mybir.AluOpType
AX = mybir.AxisListType
F32 = mybir.dt.float32
F32R = mybir.dt.float32r
BF = mybir.dt.bfloat16
bf16 = ml_dtypes.bfloat16

H = 2048
T = 2048
HT = T // 2
KT = H // 128
TT = T // 128
FI = 8192
FIT = FI // 128
CORES = 8
RND = 12582912.0
EPS = 1e-5
ISL = FI // CORES
RG = [list(range(CORES))]


class Prog:

    ENGS = ("g", "s", "t", "v", "a")

    def __init__(self, nc, blk):
        self.nc, self.blk = nc, blk
        self.pend = {e: [] for e in self.ENGS}
        self.sems, self.cnt = {}, {}

    def sem(self, name, ctx):
        self.sems[name] = ctx.enter_context(self.nc.semaphore(name))
        self.cnt[name] = 0
        return name

    SELF = {"g": "sd", "s": "sh", "t": "sp", "v": "sv", "a": "ss"}

    def op(self, eng, fn, sem=None, inc=0, waits=(), selfw=False):
        waits = tuple(waits)
        if selfw or eng in ("v", "a"):
            own = self.SELF[eng]
            if self.cnt[own] > 0:
                waits = waits + ((own, self.cnt[own]),)
        self.pend[eng].append((fn, sem, inc, waits))
        if sem is not None:
            self.cnt[sem] += inc
            return self.cnt[sem]

    def finals(self, names):
        return tuple((n, self.cnt[n]) for n in names if self.cnt[n] > 0)

    def allw(self):
        return tuple((k, v) for k, v in self.cnt.items() if v > 0)

    def flush(self):
        emap = {"g": self.blk.gpsimd, "s": self.blk.sync, "t": self.blk.tensor,
                "v": self.blk.vector, "a": self.blk.scalar}
        for e in self.ENGS:
            steps = self.pend[e]
            if not steps:
                continue

            def run(eng, steps=steps):
                for fn, sem, inc, waits in steps:
                    for wn, wv in waits:
                        if wv > 0:
                            eng.wait_ge(self.sems[wn], wv)
                    inst = fn(eng)
                    if sem is not None and inst is not None:
                        inst.then_inc(self.sems[sem], inc)
            emap[e](run)
        self.pend = {e: [] for e in self.ENGS}

    def barrier(self):
        w = self.allw()
        for e in self.ENGS:
            self.pend[e].append((lambda eng: None, None, 0, w))


def _comp_free_tree(P, SV, tile, n, scol):
    while n > 1:
        h = n // 2
        a, b = tile[:, 0:h], tile[:, h:n]
        sc, z2, t1 = tile[:, n:n + h], tile[:, n + h:n + 2 * h], tile[:, n + 2 * h:n + 3 * h]
        er = tile[:, n + 3 * h:n + 3 * h + 1]
        P.op("v", lambda e, a=a, b=b, sc=sc: e.tensor_add(sc, a, b), SV, 1)
        P.op("v", lambda e, sc=sc, a=a, z2=z2: e.tensor_sub(z2, sc, a), SV, 1)
        P.op("v", lambda e, sc=sc, z2=z2, t1=t1: e.tensor_sub(t1, sc, z2), SV, 1)
        P.op("v", lambda e, a=a, t1=t1: e.tensor_sub(t1, a, t1), SV, 1)
        P.op("v", lambda e, b=b, z2=z2: e.tensor_sub(z2, b, z2), SV, 1)
        P.op("v", lambda e, t1=t1, z2=z2: e.tensor_add(t1, t1, z2), SV, 1)
        P.op("v", lambda e, t1=t1, er=er: e.reduce_sum(er, t1, AX.X), SV, 1)
        P.op("v", lambda e, er=er, scol=scol: e.tensor_add(scol, scol, er), SV, 1)
        P.op("v", lambda e, sc=sc, a=a: e.tensor_copy(a, sc), SV, 1)
        n = h


def build_nc():
    nc = bass.Bass("TRN2", target_bir_lowering=False, debug=False)

    x_in = nc.dram_tensor("x_c", [T, H], F32, kind="ExternalInput").ap()
    W = {}
    for w in ["f", "i", "v", "q", "ro", "up", "gate", "down"]:
        W[w] = nc.dram_tensor("w_" + w, [KT, 128, KT, 128], F32,
                              kind="ExternalInput").ap()
    init_in = nc.dram_tensor("init_state", [H], F32, kind="ExternalInput").ap()
    fg_in = nc.dram_tensor("ffn_gate_sl", [ISL, H], F32, kind="ExternalInput").ap()
    fu_in = nc.dram_tensor("ffn_up_sl", [ISL, H], F32, kind="ExternalInput").ap()
    fd_in = nc.dram_tensor("ffn_down_sl", [H, ISL], F32, kind="ExternalInput").ap()
    ident_in = nc.dram_tensor("ident", [128, 128], F32, kind="ExternalInput").ap()
    masks_in = nc.dram_tensor("scan_masks", [128, 27], F32, kind="ExternalInput").ap()
    out_t = nc.dram_tensor("out_c", [T, H], F32, kind="ExternalOutput").ap()

    fg_hbm = nc.dram_tensor("fg_hbm", [2, H, T], F32).ap()
    ro_hbm = nc.dram_tensor("ro_hbm", [H, T], F32R).ap()
    hid_hbm = nc.dram_tensor("hid_hbm", [H, T], F32R).ap()
    res1_hbm = nc.dram_tensor("res1_hbm", [T, H], F32).ap()
    y1_hbm = nc.dram_tensor("y1_hbm", [T, H], F32).ap()
    xqT_hbm = nc.dram_tensor("xqT_hbm", [H, T], BF).ap()
    hqT_hbm = nc.dram_tensor("hqT_hbm", [FI, T], BF).ap()
    res2_hbm = nc.dram_tensor("res2_hbm", [T, H], F32).ap()
    wqg_sl = nc.dram_tensor("wqg_sl", [ISL, H], BF).ap()
    wqu_sl = nc.dram_tensor("wqu_sl", [ISL, H], BF).ap()
    wqd_sl = nc.dram_tensor("wqd_sl", [H, ISL], BF).ap()
    ws_in = nc.dram_tensor("ws_in", [1, 8], F32).ap()
    ws_all = nc.dram_tensor("ws_all", [CORES, 8], F32).ap()
    wqgT_sl = nc.dram_tensor("wqgT_sl", [2, 128, KT, 512], BF).ap()
    wquT_sl = nc.dram_tensor("wquT_sl", [2, 128, KT, 512], BF).ap()
    wqdT_sl = nc.dram_tensor("wqdT_sl", [KT, 128, 8, 128], BF).ap()
    wqgT_all = nc.dram_tensor("wqgT_all", [CORES, 2, 128, KT, 512], BF).ap()
    wquT_all = nc.dram_tensor("wquT_all", [CORES, 2, 128, KT, 512], BF).ap()
    wqdT_all = nc.dram_tensor("wqdT_all", [CORES, KT, 128, 8, 128], BF).ap()
    sum_in = nc.dram_tensor("sum_in", [128, 32], F32).ap()
    sum_all = nc.dram_tensor("sum_all", [CORES * 128, 32], F32).ap()

    ctx = ExitStack()
    ctx.__enter__()
    blk = ctx.enter_context(nc.Block())
    P = Prog(nc, blk)
    SD = P.sem("sd", ctx)
    SH = P.sem("sh", ctx)
    SP = P.sem("sp", ctx)
    SV = P.sem("sv", ctx)
    SS = P.sem("ss", ctx)
    CC = P.sem("cc", ctx)
    S2 = P.sem("s2", ctx)
    S3 = P.sem("s3", ctx)
    LS = [P.sem(f"ls{i_}", ctx) for i_ in range(4)]
    XT = P.sem("xt", ctx)
    M0 = P.sem("m0", ctx)
    W1 = P.sem("w1", ctx)
    W2 = P.sem("w2", ctx)
    W3 = P.sem("w3", ctx)
    W4 = P.sem("w4", ctx)
    RH = P.sem("rh", ctx)
    XS = [P.sem(f"xs{i_}", ctx) for i_ in range(2)]
    TS0 = P.sem("ts0", ctx)
    TL = [P.sem(f"tl{i_}", ctx) for i_ in range(2)]
    FS = [P.sem(f"fs{i_}", ctx) for i_ in range(2)]
    ST = [P.sem(f"st{i_}", ctx) for i_ in range(2)]
    XP = [P.sem("xp", ctx), P.sem("xp2", ctx)]
    Q1 = P.sem("q1", ctx)
    Q2 = P.sem("q2", ctx)
    Q3 = P.sem("q3", ctx)
    Q4 = P.sem("q4", ctx)
    US = [P.sem(f"us{i_}", ctx) for i_ in range(2)]
    WR = [P.sem(f"wr{i_}", ctx) for i_ in range(2)]
    WD = [P.sem(f"wd{i_}", ctx) for i_ in range(2)]

    PS = [ctx.enter_context(nc.psum_tensor(f"ps{i_}", [128, 512], F32))
          for i_ in range(8)]

    ident = ctx.enter_context(nc.sbuf_tensor("ident_t", [128, 128], F32))
    ident_bf = ctx.enter_context(nc.sbuf_tensor("ident_bf", [128, 128], BF))
    masks = ctx.enter_context(nc.sbuf_tensor("masks_t", [128, 27], F32))
    init_t = ctx.enter_context(nc.sbuf_tensor("init_t", [128, KT], F32))
    Pcol = ctx.enter_context(nc.sbuf_tensor("Pcol", [128, KT], F32))
    Scol = ctx.enter_context(nc.sbuf_tensor("Scol", [128, KT], F32))
    s_in = ctx.enter_context(nc.sbuf_tensor("s_in", [128, KT], F32))
    xs_col = ctx.enter_context(nc.sbuf_tensor("xs_col", [128, TT], F32))
    rxs_col = ctx.enter_context(nc.sbuf_tensor("rxs_col", [128, TT], F32))
    s1_col = ctx.enter_context(nc.sbuf_tensor("s1_col", [128, TT], F32))
    s2_col = ctx.enter_context(nc.sbuf_tensor("s2_col", [128, TT], F32))
    s3_col = ctx.enter_context(nc.sbuf_tensor("s3_col", [128, TT], F32))
    hs_col = ctx.enter_context(nc.sbuf_tensor("hs_col", [128, TT], F32))
    rhs_col = ctx.enter_context(nc.sbuf_tensor("rhs_col", [128, TT], F32))
    mean_col = ctx.enter_context(nc.sbuf_tensor("mean_col", [128, TT], F32))
    nmean_col = ctx.enter_context(nc.sbuf_tensor("nmean_col", [128, TT], F32))
    var_col = ctx.enter_context(nc.sbuf_tensor("var_col", [128, TT], F32))
    rstd_col = ctx.enter_context(nc.sbuf_tensor("rstd_col", [128, TT], F32))
    wsb = ctx.enter_context(nc.sbuf_tensor("wsb", [128, 12], F32))
    ones_t = ctx.enter_context(nc.sbuf_tensor("ones_t", [128, 1], F32))
    ones_r = ctx.enter_context(nc.sbuf_tensor("ones_r", [1, 128], F32))
    trst = ctx.enter_context(nc.sbuf_tensor("trst", [128, 1024], BF))

    P.op("g", lambda e: e.dma_start(out=ident[:, :], in_=ident_in), M0, 16)
    P.op("g", lambda e: e.dma_start(out=masks[:, :], in_=masks_in), M0, 16)
    P.op("g", lambda e: e.dma_start(
        out=init_t[:, :], in_=init_in.rearrange("(r p) -> p r", p=128),
        allow_slow_non_contiguous=True), M0, 16)
    m0_small = P.cnt[M0]
    P.op("v", lambda e: e.memset(ones_t[:, :], 1.0), SV, 1)
    P.op("v", lambda e: e.memset(ones_r[0:1, :], 1.0), SV, 1)
    P.op("v", lambda e: e.tensor_copy(ident_bf[:, :], ident[:, :]), SV, 1,
         waits=[(M0, m0_small)])

    xt_ctx = ExitStack()
    xt_ctx.__enter__()
    xTf = xt_ctx.enter_context(nc.sbuf_tensor("xTf", [128, KT * T], F32R))
    xstage = xt_ctx.enter_context(nc.sbuf_tensor("xstage", [128, T], F32))

    p1_ctx = ExitStack()
    p1_ctx.__enter__()
    wsl = p1_ctx.enter_context(nc.sbuf_tensor("wsl", [128, 4 * 1024], F32))
    m1 = p1_ctx.enter_context(nc.sbuf_tensor("m1", [128, 1024], F32))
    m2 = p1_ctx.enter_context(nc.sbuf_tensor("m2", [128, 1024], F32))
    wq_bf = p1_ctx.enter_context(nc.sbuf_tensor("wq_bf", [128, 4 * 1024], BF))
    parts = p1_ctx.enter_context(nc.sbuf_tensor("parts", [128, 128], F32))
    wrow = p1_ctx.enter_context(nc.sbuf_tensor("wrow", [1, 64], F32))
    wtmp = p1_ctx.enter_context(nc.sbuf_tensor("wtmp", [1, 256], F32))
    z128 = p1_ctx.enter_context(nc.sbuf_tensor("z128", [128, 3], F32))

    WSRC = ((fg_in, wqg_sl), (fu_in, wqu_sl), (fd_in, wqd_sl))

    x3 = x_in.rearrange("(t p) h -> t p h", p=128)
    p2_tr = {}
    p2_ev = {}

    def emit_p2(t0, t1):
        for t_ in range(t0, t1):
            for h2 in range(2):
                xps = S2 if h2 == 0 else S3
                P.op("g", lambda e, t_=t_, h2=h2: e.dma_start(
                    out=xstage[:, h2 * 1024:(h2 + 1) * 1024],
                    in_=x3[t_][:, h2 * 1024:(h2 + 1) * 1024]), xps, 16,
                    waits=[(SP, p2_tr.get((t_ - 1, h2), 0)),
                           (M0, m0_small if t_ == 0 and h2 == 0 else 0)])
                ldep = [(xps, P.cnt[xps])]
                for kk in range(8):
                    k = h2 * 8 + kk
                    P.op("t", lambda e, k=k: e.transpose(
                        PS[4 + k // 4][:, (k % 4) * 128:(k % 4 + 1) * 128],
                        xstage[:, k * 128:(k + 1) * 128], ident[:, :]), SP, 1,
                        waits=(tuple(ldep) + ((SS, p2_ev.get((t_ - 1, h2), 0)),))
                        if kk == 0 else ())
                p2_tr[(t_, h2)] = P.cnt[SP]
                edep = [(SP, P.cnt[SP])]
                for kk in range(8):
                    k = h2 * 8 + kk
                    P.op("a", lambda e, k=k, t_=t_: e.activation(
                        xTf[:, k * T + t_ * 128:k * T + (t_ + 1) * 128],
                        PS[4 + k // 4][:, (k % 4) * 128:(k % 4 + 1) * 128],
                        AF.Copy), SS, 1, waits=edep if kk == 0 else (),
                        selfw=True)
                p2_ev[(t_, h2)] = P.cnt[SS]

    P.op("v", lambda e: e.memset(wrow[0:1, :], 0.0), SV, 1)
    P1LD = [W1, W2, TL[0], TL[1]]
    lt_idx = 0
    red_done = {}
    red_slot_last = {}
    for wsel, (src, dst) in enumerate(WSRC):
        s3 = src.rearrange("(a p) h -> a p h", p=128)
        nt, cols = s3.shape[0], s3.shape[2]
        ch_cols = cols // 2
        ncks_h = ch_cols // 512
        for j in range(nt):
            for hh_ in range(2):
                pi = lt_idx % 4
                Wld = P1LD[pi]
                P.op("g", lambda e, j=j, hh_=hh_, s3=s3, ch_cols=ch_cols, pi=pi:
                     e.dma_start(out=wsl[:, pi * 1024:pi * 1024 + ch_cols],
                                 in_=s3[j][:, hh_ * ch_cols:(hh_ + 1) * ch_cols]),
                     Wld, 16, waits=[(SV, red_done.get(lt_idx - 4, 0))])
                dep = [(Wld, P.cnt[Wld])]
                if j == 0 and hh_ == 0:
                    dep.append((SP, P.cnt[SP]))
                for ck in range(ncks_h):
                    pc = j * (cols // 512) + hh_ * ncks_h + ck
                    P.op("v", lambda e, ck=ck, pi=pi, pc=pc:
                         e.reduce_sum(parts[:, pc:pc + 1],
                                      wsl[:, pi * 1024 + ck * 512:
                                          pi * 1024 + (ck + 1) * 512],
                                      AX.X, apply_absolute_value=True),
                         SV, 1, waits=dep if ck == 0 else ())
                red_done[lt_idx] = P.cnt[SV]
                red_slot_last[pi] = P.cnt[SV]
                lt_idx += 1
        ncols = nt * (cols // 512)
        P.op("t", lambda e, ncols=ncols: e.matmul(
            PS[0][0:1, 0:ncols], ones_t[:, 0:1], parts[:, 0:ncols],
            start=True, stop=True), SP, 1, waits=[(SV, P.cnt[SV])])
        P.op("v", lambda e, ncols=ncols: e.tensor_copy(
            wtmp[0:1, 0:ncols], PS[0][0:1, 0:ncols]), SV, 1,
            waits=[(SP, P.cnt[SP])])
        P.op("v", lambda e: e.memset(wtmp[0:1, 250:251], 0.0), SV, 1)
        _comp_free_tree(P, SV, wtmp, ncols, wtmp[0:1, 250:251])
        P.op("v", lambda e, wsel=wsel: e.tensor_copy(
            wrow[0:1, 2 * wsel:2 * wsel + 1], wtmp[0:1, 0:1]), SV, 1)
        P.op("v", lambda e, wsel=wsel: e.tensor_copy(
            wrow[0:1, 2 * wsel + 1:2 * wsel + 2], wtmp[0:1, 250:251]), SV, 1)
        emit_p2(6 * wsel, min(6 * (wsel + 1), TT))

    P.op("g", lambda e: e.dma_start(out=ws_in[0:1, 0:8], in_=wrow[0:1, 0:8]),
         M0, 16, waits=[(SV, P.cnt[SV])])
    P.op("g", lambda e: e.collective_compute(
        "AllGather", OP.bypass, replica_groups=RG,
        ins=[ws_in.opt()], outs=[ws_all.opt()]),
        CC, 1, waits=[(M0, P.cnt[M0])])
    P.op("g", lambda e: e.dma_start(
        out=m1[0:1, 0:64],
        in_=ws_all.rearrange("a b -> (a b)").rearrange("(o x) -> o x", o=1)),
        M0, 16, waits=[(CC, P.cnt[CC])])
    dep = [(M0, P.cnt[M0])]
    P.op("v", lambda e: e.memset(z128[:, :], 0.0), SV, 1, waits=dep)
    for wsel in range(3):
        for r in range(CORES):
            P.op("v", lambda e, r=r, wsel=wsel: e.tensor_copy(
                wtmp[0:1, r:r + 1],
                m1[0:1, 8 * r + 2 * wsel:8 * r + 2 * wsel + 1]), SV, 1)
            P.op("v", lambda e, r=r, wsel=wsel: e.tensor_copy(
                wtmp[0:1, 32 + r:33 + r],
                m1[0:1, 8 * r + 2 * wsel + 1:8 * r + 2 * wsel + 2]), SV, 1)
        P.op("v", lambda e: e.memset(wtmp[0:1, 250:251], 0.0), SV, 1)
        _comp_free_tree(P, SV, wtmp, 8, wtmp[0:1, 250:251])
        P.op("v", lambda e: e.reduce_sum(wtmp[0:1, 41:42], wtmp[0:1, 32:40], AX.X),
             SV, 1)
        P.op("v", lambda e: e.tensor_add(wtmp[0:1, 42:43], wtmp[0:1, 41:42],
                                         wtmp[0:1, 250:251]), SV, 1)
        P.op("v", lambda e: e.tensor_add(wtmp[0:1, 43:44], wtmp[0:1, 42:43],
                                         wtmp[0:1, 0:1]), SV, 1)
        P.op("v", lambda e: e.tensor_scalar(
            wtmp[0:1, 44:45], wtmp[0:1, 43:44], float(2.0 ** -24), EPS,
            OP.mult, OP.add), SV, 1)
        P.op("v", lambda e, wsel=wsel: e.tensor_copy(
            z128[0:1, wsel:wsel + 1], wtmp[0:1, 44:45]), SV, 1)
    P.op("t", lambda e: e.matmul(PS[1][:, 0:3], ones_r[0:1, 0:128],
                                 z128[0:1, 0:3], start=True, stop=True),
         SP, 1, waits=[(SV, P.cnt[SV])])
    P.op("v", lambda e: e.tensor_copy(wsb[:, 0:3], PS[1][:, 0:3]), SV, 1,
         waits=[(SP, P.cnt[SP])])
    P.op("v", lambda e: e.tensor_scalar_mul(wsb[:, 3:6], wsb[:, 0:3], 0.5), SV, 1)
    P.op("v", lambda e: e.tensor_scalar_mul(wsb[:, 6:9], wsb[:, 3:6], -1.0), SV, 1)
    sv_wsb = P.cnt[SV]
    p2_ss = P.cnt[SS]

    QLD = [Q1, Q2, W3, W4]
    QST = [Q3, XP[0], Q4, XP[1]]
    qt_idx = 0
    cmp_done = {}
    st_done = {}
    w_after = {}
    for wsel, (src, dst) in enumerate(WSRC):
        s3 = src.rearrange("(a p) h -> a p h", p=128)
        d3 = dst.rearrange("(a p) h -> a p h", p=128)
        nt, cols = s3.shape[0], s3.shape[2]
        ch_cols = cols // 2
        for j in range(nt):
            for hh_ in range(2):
                pi = qt_idx % 4
                Wld = QLD[pi]
                Wst = QST[pi]
                P.op("s", lambda e, j=j, hh_=hh_, s3=s3, ch_cols=ch_cols, pi=pi:
                     e.dma_start(out=wsl[:, pi * 1024:pi * 1024 + ch_cols],
                                 in_=s3[j][:, hh_ * ch_cols:(hh_ + 1) * ch_cols]),
                     Wld, 16,
                     waits=[(SV, cmp_done.get(qt_idx - 4,
                                              red_slot_last.get(pi, 0)))])
                depL = [(Wld, P.cnt[Wld])] + \
                    ([(SV, sv_wsb)] if qt_idx == 0 else [])
                P.op("v", lambda e, ch_cols=ch_cols, wsel=wsel, pi=pi:
                     e.tensor_scalar(
                         m1[:, 0:ch_cols], wsl[:, pi * 1024:pi * 1024 + ch_cols],
                         wsb[:, 3 + wsel:4 + wsel], None, OP.is_ge),
                     SV, 1, waits=depL)
                P.op("v", lambda e, ch_cols=ch_cols, wsel=wsel, pi=pi:
                     e.tensor_scalar(
                         m2[:, 0:ch_cols], wsl[:, pi * 1024:pi * 1024 + ch_cols],
                         wsb[:, 6 + wsel:7 + wsel], None, OP.is_le), SV, 1)
                P.op("v", lambda e, ch_cols=ch_cols, pi=pi: e.tensor_sub(
                    wq_bf[:, pi * 1024:pi * 1024 + ch_cols],
                    m1[:, 0:ch_cols], m2[:, 0:ch_cols]),
                    SV, 1, waits=[(QST[pi], st_done.get(qt_idx - 4, 0))])
                cmp_done[qt_idx] = P.cnt[SV]
                P.op("s", lambda e, j=j, hh_=hh_, d3=d3, ch_cols=ch_cols, pi=pi:
                     e.dma_start(
                         out=d3[j][:, hh_ * ch_cols:(hh_ + 1) * ch_cols],
                         in_=wq_bf[:, pi * 1024:pi * 1024 + ch_cols]), Wst, 16,
                     waits=[(SV, P.cnt[SV])])
                st_done[qt_idx] = P.cnt[Wst]
                qt_idx += 1
        w_after[wsel] = tuple((s, P.cnt[s]) for s in QST)
    sv_quant = P.cnt[SV]

    for wsel, (srcT, dstT, rows, colw) in enumerate(
            ((wqg_sl, wqgT_sl, 1024, H), (wqu_sl, wquT_sl, 1024, H),
             (wqd_sl, wqdT_sl, H, ISL))):
        first = True
        for hh in range(colw // 128):
            for rh in range(rows // 1024):
                wv = [(TS0, P.cnt[TS0])]
                if first:
                    wv += list(w_after[wsel])
                    first = False
                P.op("s", lambda e, srcT=srcT, hh=hh, rh=rh: e.dma_start(
                    out=trst[:, :],
                    in_=srcT[rh * 1024:(rh + 1) * 1024, hh * 128:(hh + 1) * 128],
                    transpose=True), XS[0], 16, waits=wv)
                if wsel < 2:
                    for io2 in range(2):
                        P.op("s", lambda e, dstT=dstT, hh=hh, io2=io2:
                             e.dma_start(
                                 out=dstT[io2, :, hh, :],
                                 in_=trst[:, io2 * 512:(io2 + 1) * 512]),
                             TS0, 16, waits=[(XS[0], P.cnt[XS[0]])])
                else:
                    for mt in range(8):
                        P.op("s", lambda e, dstT=dstT, hh=hh, rh=rh, mt=mt:
                             e.dma_start(
                                 out=dstT[rh * 8 + mt, :, hh, :],
                                 in_=trst[:, mt * 128:(mt + 1) * 128]),
                             TS0, 16, waits=[(XS[0], P.cnt[XS[0]])])
    TS0_WQT = P.cnt[TS0]

    P.flush()
    p1_ctx.__exit__(None, None, None)

    ab_ctx = ExitStack()
    ab_ctx.__enter__()
    fbuf = ab_ctx.enter_context(nc.sbuf_tensor("fbuf", [128, T], F32))
    gbuf = ab_ctx.enter_context(nc.sbuf_tensor("gbuf", [128, T], F32))
    sgst = ab_ctx.enter_context(nc.sbuf_tensor("sgst", [128, T], F32))
    svst = ab_ctx.enter_context(nc.sbuf_tensor("svst", [128, T], F32))
    zeros = ab_ctx.enter_context(nc.sbuf_tensor("zeros_t", [128, 256], F32))
    wsl3 = ab_ctx.enter_context(nc.sbuf_tensor("wsl3", [128, 2 * KT * 128], F32R))
    sall = ab_ctx.enter_context(nc.sbuf_tensor("sall", [128, CORES * 32], F32))
    cmp = ab_ctx.enter_context(nc.sbuf_tensor("cmp", [128, 12 * KT], F32))

    pa_gate = P.finals(["w1", "w2", "w3", "w4", "tl0", "tl1", "q1", "q2",
                        "q3", "q4", "xp", "xp2", "m0"]) + ((SV, sv_quant),)
    P.op("v", lambda e: e.memset(zeros[:, :], 0.0), SV, 1, waits=pa_gate)

    def load_wslab(w, m, slot, waits):
        src = W[w][m]
        dstw = wsl3[:, slot * KT * 128:(slot + 1) * KT * 128].rearrange(
            "p (k m) -> p k m", k=KT)
        P.op("g", lambda e, src=src, dstw=dstw: e.dma_start(out=dstw, in_=src),
             LS[slot], 16, waits=waits)
        return (LS[slot], P.cnt[LS[slot]])

    def mm_group(slot, bs, waits):
        lhs_slab = wsl3[:, slot * KT * 128:(slot + 1) * KT * 128]
        for k in range(KT):
            for n in range(4):
                P.op("t", lambda e, k=k, n=n, bs=bs, lhs_slab=lhs_slab:
                     e.matmul(PS[bs + n][:, :],
                              lhs_slab[:, k * 128:(k + 1) * 128],
                              xTf[:, k * T + n * 512:k * T + (n + 1) * 512],
                              start=(k == 0), stop=(k == KT - 1)),
                     SP, 1, waits=waits if (k == 0 and n == 0) else ())
        return P.cnt[SP]

    slab_free = {}
    bank_free = {("s", 0): p2_ss, ("s", 1): p2_ss, ("v", 0): sv_wsb}
    stage_v = {}
    fstore = {}
    gstore = {}
    gidx = 0
    for r in range(KT):
        spar = S2 if r % 2 == 0 else S3
        for gi, w in enumerate(("f", "i", "v")):
            slot = gidx % 2
            wvl = [(SP, slab_free.get(slot, 0))]
            if gidx == 0:
                wvl += list(pa_gate)
            dmadep = load_wslab(w, r, slot, wvl)
            bs = 4 * (gidx % 2)
            wv = [dmadep, (SS, bank_free.get(("s", gidx % 2), 0)),
                  (SV, bank_free.get(("v", gidx % 2), 0))]
            spc = mm_group(slot, bs, wv)
            slab_free[slot] = spc
            dstt, fn = {"f": (fbuf, AF.Tanh), "i": (sgst, AF.Sigmoid),
                        "v": (svst, AF.Sigmoid)}[w]
            mmdep = [(SP, spc), (SV, stage_v.get(r - 1, 0))]
            if w == "f" and r >= 1:
                mmdep.append(fstore[r - 1])
            for n in range(4):
                P.op("a", lambda e, dstt=dstt, n=n, bs=bs, fn=fn:
                     e.activation(dstt[:, n * 512:(n + 1) * 512],
                                  PS[bs + n][:, :], fn),
                     SS, 1, waits=mmdep if n == 0 else (), selfw=True)
            bank_free[("s", gidx % 2)] = P.cnt[SS]
            if w == "v":
                for n in range(4):
                    P.op("v", lambda e, n=n, bs=bs: e.tensor_mul(
                        svst[:, n * 512:(n + 1) * 512], PS[bs + n][:, :],
                        svst[:, n * 512:(n + 1) * 512]),
                        SV, 1, waits=[(SS, P.cnt[SS])] if n == 0 else ())
                bank_free[("v", gidx % 2)] = P.cnt[SV]
                gw = [gstore[r - 1]] if r >= 1 else []
                P.op("v", lambda e: e.tensor_mul(gbuf[:, :], sgst[:, :],
                                                 svst[:, :]), SV, 1, waits=gw)
            gidx += 1
        P.op("g", lambda e, r=r: e.dma_start(
            out=fg_hbm[0, r * 128:(r + 1) * 128, :], in_=fbuf[:, :]), spar, 16,
            waits=[(SS, bank_free[("s", (gidx - 3) % 2)])])
        P.op("g", lambda e, r=r: e.dma_start(
            out=fg_hbm[1, r * 128:(r + 1) * 128, :], in_=gbuf[:, :]), spar, 16,
            waits=[(SV, P.cnt[SV])])
        fstore[r] = (spar, P.cnt[spar])
        gstore[r] = (spar, P.cnt[spar])
        P.op("v", lambda e: e.tensor_tensor_scan(
            sgst[:, :], fbuf[:, :], gbuf[:, :], 0.0, OP.mult, OP.add), SV, 1)
        P.op("v", lambda e, r=r: e.tensor_copy(
            Scol[:, r:r + 1], sgst[:, T - 1:T]), SV, 1)
        for c in range(8):
            initP = 1.0 if c == 0 else svst[:, c * 256 - 1:c * 256]
            P.op("v", lambda e, c=c, initP=initP: e.tensor_tensor_scan(
                svst[:, c * 256:(c + 1) * 256], fbuf[:, c * 256:(c + 1) * 256],
                zeros[:, :], initP, OP.mult, OP.add), SV, 1)
        P.op("v", lambda e, r=r: e.tensor_copy(
            Pcol[:, r:r + 1], svst[:, T - 1:T]), SV, 1)
        stage_v[r] = P.cnt[SV]

    P.op("g", lambda e: e.dma_start(out=sum_in[:, 0:KT], in_=Scol[:, :]),
         M0, 16, waits=[(SV, P.cnt[SV])])
    P.op("g", lambda e: e.dma_start(out=sum_in[:, KT:2 * KT], in_=Pcol[:, :]), M0, 16)
    P.op("g", lambda e: e.collective_compute(
        "AllGather", OP.bypass, replica_groups=RG,
        ins=[sum_in.opt()], outs=[sum_all.opt()]),
        CC, 1, waits=[(M0, P.cnt[M0])])
    AG3_DONE = P.cnt[CC]

    B4 = [fbuf, gbuf, sgst, svst]
    pa_v, pa_s = P.cnt[SV], P.cnt[SS]
    lb_free = {("u", 0): bank_free[("v", 0)], ("u", 1): bank_free.get(("v", 1), 0),
               ("g", 0): bank_free[("v", 0)], ("g", 1): bank_free.get(("v", 1), 0)}
    lb_sfree = {0: bank_free[("s", 0)], 1: bank_free[("s", 1)]}
    hid_done = {}
    st_cnt = {}
    ab_use = {}
    for l in range(KT):
        p = l % 2
        A, Bb = B4[2 * p], B4[2 * p + 1]
        du = load_wslab("up", l, 0, [(SP, slab_free.get(0, 0))])
        dg = load_wslab("gate", l, 1, [(SP, slab_free.get(1, 0))])
        mmc = {}
        for w_, dep0, slotw in (("u", du, 0), ("g", dg, 1)):
            for nh in range(2):
                bpair = {("u", 0): 0, ("u", 1): 4, ("g", 0): 2, ("g", 1): 6}[(w_, nh)]
                wv = [dep0, (SV, lb_free[(w_, nh)]), (SS, lb_sfree[nh])]
                for k in range(KT):
                    for ni in range(2):
                        P.op("t", lambda e, k=k, ni=ni, bpair=bpair, nh=nh, slotw=slotw:
                             e.matmul(PS[bpair + ni][:, :],
                                      wsl3[:, slotw * KT * 128 + k * 128:
                                           slotw * KT * 128 + (k + 1) * 128],
                                      xTf[:, k * T + nh * 1024 + ni * 512:
                                          k * T + nh * 1024 + (ni + 1) * 512],
                                      start=(k == 0), stop=(k == KT - 1)),
                             SP, 1, waits=wv if (k == 0 and ni == 0) else ())
                mmc[(w_, nh)] = P.cnt[SP]
            slab_free[slotw] = P.cnt[SP]
        for nh in range(2):
            gpair = 2 + 4 * nh
            upair = 4 * nh
            sdep = [(SP, mmc[("g", nh)]),
                    (SV, ab_use.get((p, "A"), pa_v if l < 2 else 0))]
            if l < 2:
                sdep.append((SV, pa_v))
                sdep.append(fstore[KT - 1])
            for ni in range(2):
                P.op("a", lambda e, ni=ni, gpair=gpair, nh=nh, A=A: e.activation(
                    A[:, nh * 1024 + ni * 512:nh * 1024 + (ni + 1) * 512],
                    PS[gpair + ni][:, :], AF.Sigmoid),
                    SS, 1, waits=sdep if ni == 0 else (), selfw=True)
            lb_sfree[nh] = P.cnt[SS]
            vdep = [(SS, P.cnt[SS])]
            for ni in range(2):
                P.op("v", lambda e, ni=ni, gpair=gpair, nh=nh, A=A: e.tensor_mul(
                    A[:, nh * 1024 + ni * 512:nh * 1024 + (ni + 1) * 512],
                    PS[gpair + ni][:, :],
                    A[:, nh * 1024 + ni * 512:nh * 1024 + (ni + 1) * 512]),
                    SV, 1, waits=vdep if ni == 0 else ())
            lb_free[("g", nh)] = P.cnt[SV]
            bw = [(ST[p], st_cnt.get(l - 2, 0))] if nh == 0 else []
            if l < 2 and nh == 0:
                bw.append((SV, pa_v))
                bw.append(gstore[KT - 1])
            for ni in range(2):
                P.op("v", lambda e, ni=ni, upair=upair, nh=nh, A=A, Bb=Bb:
                     e.tensor_mul(
                         Bb[:, nh * 1024 + ni * 512:nh * 1024 + (ni + 1) * 512],
                         PS[upair + ni][:, :],
                         A[:, nh * 1024 + ni * 512:nh * 1024 + (ni + 1) * 512]),
                     SV, 1, waits=bw if ni == 0 else ())
            lb_free[("u", nh)] = P.cnt[SV]
        ab_use[(p, "A")] = P.cnt[SV]
        P.op("s", lambda e, l=l, Bb=Bb: e.dma_start(
            out=hid_hbm[l * 128:(l + 1) * 128, :].bitcast(F32), in_=Bb[:, :]),
            ST[p], 16,
            waits=[(SV, P.cnt[SV])] + ([(TS0, P.cnt[TS0])] if l == 0 else []))
        st_cnt[l] = P.cnt[ST[p]]
        hid_done[l] = (ST[p], st_cnt[l])

    for (i_t, o_t) in ((wqgT_sl, wqgT_all), (wquT_sl, wquT_all),
                       (wqdT_sl, wqdT_all)):
        P.op("g", lambda e, i_t=i_t, o_t=o_t: e.collective_compute(
            "AllGather", OP.bypass, replica_groups=RG,
            ins=[i_t.opt()], outs=[o_t.opt()]),
            CC, 1, waits=[(TS0, TS0_WQT)])
    AG2_DONE = P.cnt[CC]

    P.op("g", lambda e: e.dma_start(
        out=sall[:, :].rearrange("p (r c) -> p r c", r=CORES),
        in_=sum_all.rearrange("(r p) c -> p r c", p=128)), M0, 16,
        waits=[(CC, AG3_DONE)])
    m0_sall = P.cnt[M0]
    P.op("v", lambda e: e.memset(cmp[:, :], 0.0), SV, 1, waits=[(M0, m0_sall)])
    for kk in (1, 2, 3):
        for r in range(CORES):
            srcP = sall[:, r * 32 + KT:r * 32 + 2 * KT]
            srcS = sall[:, r * 32:r * 32 + KT]
            mcol = masks[:, (kk - 1) * 8 + r:(kk - 1) * 8 + r + 1]
            sp_, dp_ = (r % 2) * KT, ((r + 1) % 2) * KT
            P.op("v", lambda e, srcP=srcP, mcol=mcol, sp_=sp_, dp_=dp_:
                 e.scalar_tensor_tensor(cmp[:, dp_:dp_ + KT], srcP, mcol,
                                        cmp[:, sp_:sp_ + KT], OP.mult, OP.add),
                 SV, 1)
            sp2, dp2 = 2 * KT + (r % 2) * KT, 2 * KT + ((r + 1) % 2) * KT
            P.op("v", lambda e, srcS=srcS, mcol=mcol, sp2=sp2, dp2=dp2:
                 e.scalar_tensor_tensor(cmp[:, dp2:dp2 + KT], srcS, mcol,
                                        cmp[:, sp2:sp2 + KT], OP.mult, OP.add),
                 SV, 1)
        P.op("v", lambda e, kk=kk: e.tensor_scalar(
            cmp[:, (3 + kk) * KT:(4 + kk) * KT], cmp[:, 0:KT],
            masks[:, 23 + kk:24 + kk], None, OP.add), SV, 1)
        P.op("v", lambda e, kk=kk: e.tensor_copy(
            cmp[:, (6 + kk) * KT:(7 + kk) * KT], cmp[:, 2 * KT:3 * KT]), SV, 1)
        P.op("v", lambda e: e.memset(cmp[:, 0:4 * KT], 0.0), SV, 1)
    tt_ = cmp[:, 10 * KT:11 * KT]
    P.op("v", lambda e, tt_=tt_: e.tensor_mul(tt_, cmp[:, 6 * KT:7 * KT],
                                              init_t[:, :]), SV, 1)
    P.op("v", lambda e, tt_=tt_: e.tensor_add(tt_, tt_, cmp[:, 9 * KT:10 * KT]),
         SV, 1)
    P.op("v", lambda e, tt_=tt_: e.tensor_mul(tt_, cmp[:, 5 * KT:6 * KT], tt_),
         SV, 1)
    P.op("v", lambda e, tt_=tt_: e.tensor_add(tt_, tt_, cmp[:, 8 * KT:9 * KT]),
         SV, 1)
    P.op("v", lambda e, tt_=tt_: e.tensor_mul(s_in[:, :], cmp[:, 4 * KT:5 * KT],
                                              tt_), SV, 1)
    P.op("v", lambda e: e.tensor_add(s_in[:, :], s_in[:, :],
                                     cmp[:, 7 * KT:8 * KT]), SV, 1)

    ro_done = {}
    q_free = {0: lb_free[("u", 0)], 1: lb_free[("u", 1)]}
    fs_cnt = {}
    ro_v = {}
    for r in range(KT):
        p = r % 2
        A, Bb = B4[2 * p], B4[2 * p + 1]
        fw = [(FS[p], P.cnt[FS[p]]), (SV, ab_use.get((p, "A"), 0)), fstore[r]]
        if r >= 2:
            fw.append(ro_done[r - 2])
        P.op("s", lambda e, r=r, A=A: e.dma_start(
            out=A[:, :], in_=fg_hbm[0, r * 128:(r + 1) * 128, :]), FS[p], 16,
            waits=fw)
        gw2 = [(ST[p], st_cnt.get(15 - (1 - p), 0)), gstore[r]]
        if r >= 2:
            gw2.append((SV, ro_v[r - 2]))
        P.op("s", lambda e, r=r, Bb=Bb: e.dma_start(
            out=Bb[:, :], in_=fg_hbm[1, r * 128:(r + 1) * 128, :]), FS[p], 16,
            waits=gw2)
        fs_cnt[r] = P.cnt[FS[p]]
        slot = r % 2
        dmadep = load_wslab("q", r, slot, [(SP, slab_free.get(slot, 0))])
        bs = 4 * (r % 2)
        wv = [dmadep, (SV, q_free[r % 2]), (SS, lb_sfree.get(r % 2, 0))]
        spc = mm_group(slot, bs, wv)
        slab_free[slot] = spc
        lb_sfree[r % 2] = 0
        sw = [(FS[p], fs_cnt[r])]
        if r == 0:
            sw.append((SP, p2_tr[(TT - 1, 1)]))
        P.op("v", lambda e, A=A, Bb=Bb, r=r: e.tensor_tensor_scan(
            xstage[:, :], A[:, :], Bb[:, :], s_in[:, r:r + 1],
            OP.mult, OP.add), SV, 1, waits=sw)
        for n in range(4):
            P.op("v", lambda e, n=n, bs=bs, A=A: e.tensor_mul(
                A[:, n * 512:(n + 1) * 512], PS[bs + n][:, :],
                xstage[:, n * 512:(n + 1) * 512]),
                SV, 1, waits=[(SP, spc)] if n == 0 else ())
        q_free[r % 2] = P.cnt[SV]
        P.op("a", lambda e, A=A, Bb=Bb: e.activation(
            Bb[:, :], A[:, :], AF.Sigmoid), SS, 1,
            waits=[(SV, P.cnt[SV])], selfw=True)
        P.op("v", lambda e, A=A, Bb=Bb: e.tensor_mul(A[:, :], Bb[:, :], A[:, :]),
             SV, 1, waits=[(SS, P.cnt[SS])])
        ro_v[r] = P.cnt[SV]
        ab_use[(p, "A")] = P.cnt[SV]
        spar = S2 if p == 0 else S3
        P.op("g", lambda e, r=r, A=A: e.dma_start(
            out=ro_hbm[r * 128:(r + 1) * 128, :], in_=A[:, :]), spar, 16,
            waits=[(SV, P.cnt[SV])])
        ro_done[r] = (spar, P.cnt[spar])
    P.flush()
    ab_ctx.__exit__(None, None, None)
    xt_ctx.__exit__(None, None, None)

    pb_done = P.finals(["sp", "sv", "ss", "s2", "s3", "fs0", "fs1",
                        "st0", "st1", "ls0", "ls1"])
    pb_sp, pb_sv, pb_ss = P.cnt[SP], P.cnt[SV], P.cnt[SS]
    res1_done = {}
    with nc.sbuf_tensor("rhsb", [128, 32 * HT], F32R) as rhsb, \
         nc.sbuf_tensor("wro2", [128, 2 * KT * 128], F32R) as wro2, \
         nc.sbuf_tensor("wdn2", [128, 2 * KT * 128], F32R) as wdn2, \
         nc.sbuf_tensor("tstage", [128, HT], F32) as tstage, \
         nc.sbuf_tensor("xstrip", [128, 2 * 8 * 128], F32) as xstrip, \
         nc.sbuf_tensor("rstage", [128, 2 * 8 * 128], F32) as rstage:
        x3j = x_in.rearrange("(j p) h -> p j h", p=128)
        r13 = res1_hbm.rearrange("(j p) h -> p j h", p=128)
        roT3 = ro_hbm.rearrange("(k p) t -> k p t", p=128)
        hiT3 = hid_hbm.rearrange("(k p) t -> k p t", p=128)
        mm_done = {-1: pb_sp}
        ev_done = {}
        tr_done = {}
        pend_tr = None
        for half in range(2):
            for k in range(KT):
                wv = [hid_done[k]]
                if half == 0 and k == 0:
                    wv += list(pb_done)
                if half == 1 and k == 0:
                    wv.append((SP, mm_done[0]))
                P.op("s", lambda e, k=k, half=half, hiT3=hiT3: e.dma_start(
                    out=rhsb[:, k * HT:(k + 1) * HT],
                    in_=hiT3[k][:, half * HT:(half + 1) * HT]), RH, 16, waits=wv)
                P.op("s", lambda e, k=k, half=half, roT3=roT3: e.dma_start(
                    out=rhsb[:, (KT + k) * HT:(KT + k + 1) * HT],
                    in_=roT3[k][:, half * HT:(half + 1) * HT]), RH, 16,
                    waits=[ro_done[k]])
            rdep = (RH, P.cnt[RH])
            for m in range(KT):
                sl = m % 2
                for wi, (wt, dstw) in enumerate(((W["down"], wdn2), (W["ro"], wro2))):
                    src = wt[m]
                    d = dstw[:, sl * KT * 128:(sl + 1) * KT * 128].rearrange(
                        "p (k m) -> p k m", k=KT)
                    wlw = [(SP, mm_done.get((half, m - 2),
                                            mm_done.get(half - 1, 0)))]
                    if half == 0 and m == 0:
                        wlw += list(pb_done)
                    if wi == 0:
                        P.op("g", lambda e, src=src, d=d: e.dma_start(
                            out=d, in_=src), LS[sl], 16, waits=wlw)
                    else:
                        P.op("s", lambda e, src=src, d=d: e.dma_start(
                            out=d, in_=src.bitcast(F32R)), WR[sl], 16, waits=wlw)
                xsw = [(SV, ev_done.get((half, m - 2),
                                        ev_done.get((half - 1, KT - 2 + m % 2), 0)))]
                if half == 0 and m == 0:
                    xsw += list(pb_done)
                P.op("s", lambda e, m=m, half=half, sl=sl, x3j=x3j: e.dma_start(
                    out=xstrip[:, sl * 1024:(sl + 1) * 1024].rearrange(
                        "p (j m) -> p j m", j=8),
                    in_=x3j[:, half * 8:(half + 1) * 8, m * 128:(m + 1) * 128]),
                    XS[sl], 16, waits=xsw)
                wdep = (LS[sl], P.cnt[LS[sl]])
                wdep2 = (WR[sl], P.cnt[WR[sl]])
                bs = 2 * (m % 2)
                wv = [rdep, wdep, wdep2, (SS, ev_done.get(("s", half, m - 2), 0))]
                if half == 0 and m == 0:
                    wv += [(SV, pb_sv), (SS, pb_ss)]
                for j in range(2 * KT):
                    lhs = (wdn2 if j < KT else wro2)[
                        :, sl * KT * 128 + (j % KT) * 128:sl * KT * 128 + (j % KT + 1) * 128]
                    for n in range(2):
                        P.op("t", lambda e, j=j, n=n, bs=bs, lhs=lhs: e.matmul(
                            PS[bs + n][:, :], lhs,
                            rhsb[:, j * HT + n * 512:j * HT + (n + 1) * 512],
                            start=(j == 0), stop=(j == 2 * KT - 1)),
                            SP, 1, waits=wv if (j == 0 and n == 0) else ())
                mm_done[(half, m)] = P.cnt[SP]
                mm_done[half] = P.cnt[SP]
                if pend_tr is not None:
                    pend_tr()
                mmdep = [(SP, P.cnt[SP])]
                for n in range(2):
                    P.op("a", lambda e, n=n, bs=bs: e.activation(
                        tstage[:, n * 512:(n + 1) * 512], PS[bs + n][:, :], AF.Copy),
                        SS, 1, waits=mmdep if n == 0 else (), selfw=True)
                ev_done[("s", half, m)] = P.cnt[SS]
                sdep = [(SS, P.cnt[SS]),
                        (SV, tr_done.get((half, m - 2),
                                         tr_done.get((half - 1, KT - 2 + m % 2), 0)))]
                tbs = 4 + 2 * (m % 2)

                def emit_tr(m=m, half=half, sl=sl, sdep=tuple(sdep), tbs=tbs):
                    for j in range(8):
                        P.op("t", lambda e, j=j, tbs=tbs: e.transpose(
                            PS[tbs + j // 4][:, (j % 4) * 128:(j % 4 + 1) * 128],
                            tstage[:, j * 128:(j + 1) * 128], ident[:, :]),
                            SP, 1, waits=sdep if j == 0 else ())
                    spar = S2 if sl == 0 else S3
                    trdep = [(SP, P.cnt[SP]), (XS[sl], P.cnt[XS[sl]]),
                             (spar, P.cnt[spar])]
                    for j in range(8):
                        P.op("v", lambda e, j=j, sl=sl, tbs=tbs: e.tensor_add(
                            rstage[:, sl * 1024 + j * 128:sl * 1024 + (j + 1) * 128],
                            PS[tbs + j // 4][:, (j % 4) * 128:(j % 4 + 1) * 128],
                            xstrip[:, sl * 1024 + j * 128:sl * 1024 + (j + 1) * 128]),
                            SV, 1, waits=trdep if j == 0 else ())
                    tr_done[(half, m)] = P.cnt[SV]
                    ev_done[(half, m)] = P.cnt[SV]
                    P.op("g", lambda e, m=m, half=half, sl=sl: e.dma_start(
                        out=r13[:, half * 8:(half + 1) * 8, m * 128:(m + 1) * 128],
                        in_=rstage[:, sl * 1024:(sl + 1) * 1024].rearrange(
                            "p (j m) -> p j m", j=8)), spar, 16,
                        waits=[(SV, P.cnt[SV])])
                    res1_done[half] = (spar, P.cnt[spar])
                pend_tr = emit_tr
            pend_tr()
            pend_tr = None
        P.flush()

    pc_done = P.finals(["sp", "sv", "ss", "s2", "s3", "rh", "ls0", "ls1",
                        "ls2", "ls3", "wr0", "wr1", "xs0", "xs1"])
    pc_ss0 = P.cnt[SS]
    xqst_done = {}
    with nc.sbuf_tensor("ln_t", [128, 2 * H], F32) as ln_t, \
         nc.sbuf_tensor("ln_s", [128, 2 * H], F32) as ln_s, \
         nc.sbuf_tensor("qbf", [128, 2 * H], BF) as qbf, \
         nc.sbuf_tensor("xqtst", [128, 2 * KT * 128], BF) as xqtst:
        r1t = res1_hbm.rearrange("(t p) h -> t p h", p=128)
        y1t = y1_hbm.rearrange("(t p) h -> t p h", p=128)
        xqT3 = xqT_hbm.rearrange("(k p) t -> p k t", p=128)
        sq_done = {}
        vq_done = {}
        y1_st = {}
        trq_done = {}
        for t_ in range(TT):
            buf = t_ % 2
            spar = S2 if buf == 0 else S3
            lw = [(TL[buf], P.cnt[TL[buf]]), (SS, sq_done.get(t_ - 2, 0)),
                  (SV, vq_done.get(t_ - 2, 0))]
            if t_ < 2:
                lw += list(pc_done)
            P.op("g", lambda e, t_=t_, buf=buf, r1t=r1t: e.dma_start(
                out=ln_t[:, buf * H:(buf + 1) * H], in_=r1t[t_]), TL[buf], 16,
                waits=lw)
            dep = [(TL[buf], P.cnt[TL[buf]])]
            P.op("v", lambda e, t_=t_, buf=buf: e.reduce_sum(
                mean_col[:, t_:t_ + 1], ln_t[:, buf * H:(buf + 1) * H], AX.X),
                SV, 1, waits=dep)
            P.op("v", lambda e, t_=t_: e.tensor_single_scalar(
                mean_col[:, t_:t_ + 1], mean_col[:, t_:t_ + 1], 1.0 / H, OP.mult),
                SV, 1)
            P.op("v", lambda e, t_=t_: e.tensor_single_scalar(
                nmean_col[:, t_:t_ + 1], mean_col[:, t_:t_ + 1], -1.0, OP.mult),
                SV, 1)
            sqw = [(SV, P.cnt[SV])] + ([y1_st[t_ - 2]] if t_ >= 2 else [])
            P.op("a", lambda e, t_=t_, buf=buf: e.activation(
                ln_s[:, buf * H:(buf + 1) * H], ln_t[:, buf * H:(buf + 1) * H],
                AF.Square, bias=nmean_col[:, t_:t_ + 1], scale=1.0,
                accum_out=var_col[:, t_:t_ + 1]),
                SS, 1, waits=sqw, selfw=True)
            P.op("v", lambda e, t_=t_: e.tensor_scalar(
                var_col[:, t_:t_ + 1], var_col[:, t_:t_ + 1], 1.0 / H, EPS,
                OP.mult, OP.add), SV, 1, waits=[(SS, P.cnt[SS])])
            P.op("v", lambda e, t_=t_: e.reciprocal(
                rstd_col[:, t_:t_ + 1], var_col[:, t_:t_ + 1]), SV, 1)
            P.op("a", lambda e, t_=t_: e.activation(
                rstd_col[:, t_:t_ + 1], rstd_col[:, t_:t_ + 1], AF.Sqrt),
                SS, 1, waits=[(SV, P.cnt[SV])], selfw=True)
            P.op("v", lambda e, t_=t_: e.tensor_mul(
                nmean_col[:, t_:t_ + 1], rstd_col[:, t_:t_ + 1],
                rstd_col[:, t_:t_ + 1]), SV, 1, waits=[(SS, P.cnt[SS])])
            P.op("v", lambda e, t_=t_: e.tensor_mul(
                nmean_col[:, t_:t_ + 1], nmean_col[:, t_:t_ + 1],
                var_col[:, t_:t_ + 1]), SV, 1)
            P.op("v", lambda e, t_=t_: e.tensor_scalar(
                nmean_col[:, t_:t_ + 1], nmean_col[:, t_:t_ + 1], -0.5, 1.5,
                OP.mult, OP.add), SV, 1)
            P.op("v", lambda e, t_=t_: e.tensor_mul(
                rstd_col[:, t_:t_ + 1], rstd_col[:, t_:t_ + 1],
                nmean_col[:, t_:t_ + 1]), SV, 1)
            yw = [y1_st[t_ - 2]] if t_ >= 2 else []
            P.op("v", lambda e, t_=t_, buf=buf: e.tensor_scalar(
                ln_s[:, buf * H:(buf + 1) * H], ln_t[:, buf * H:(buf + 1) * H],
                mean_col[:, t_:t_ + 1], rstd_col[:, t_:t_ + 1],
                OP.subtract, OP.mult), SV, 1, waits=yw, selfw=True)
            P.op("v", lambda e, t_=t_, buf=buf: e.reduce_max(
                xs_col[:, t_:t_ + 1], ln_s[:, buf * H:(buf + 1) * H], AX.X,
                apply_absolute_value=True), SV, 1)
            P.op("g", lambda e, t_=t_, buf=buf, y1t=y1t: e.dma_start(
                out=y1t[t_], in_=ln_s[:, buf * H:(buf + 1) * H]), spar, 16,
                waits=[(SV, P.cnt[SV])])
            y1_st[t_] = (spar, P.cnt[spar])
            sq_done[t_] = P.cnt[SS]
            P.op("v", lambda e, t_=t_: e.tensor_scalar(
                xs_col[:, t_:t_ + 1], xs_col[:, t_:t_ + 1],
                float(np.float32(1.0) / np.float32(127.0)), EPS,
                OP.mult, OP.add), SV, 1)
            P.op("v", lambda e, t_=t_: e.reciprocal(
                rxs_col[:, t_:t_ + 1], xs_col[:, t_:t_ + 1]), SV, 1)
            P.op("v", lambda e, t_=t_, buf=buf: e.tensor_scalar(
                ln_t[:, buf * H:(buf + 1) * H], ln_s[:, buf * H:(buf + 1) * H],
                rxs_col[:, t_:t_ + 1], RND, OP.mult, OP.add), SV, 1, selfw=True)
            qw = [(SP, trq_done.get(t_ - 2, 0))]
            P.op("v", lambda e, t_=t_, buf=buf: e.tensor_single_scalar(
                qbf[:, buf * H:(buf + 1) * H], ln_t[:, buf * H:(buf + 1) * H],
                -RND, OP.add), SV, 1, waits=qw, selfw=True)
            vq_done[t_] = P.cnt[SV]
            for wv_ in range(2):
                tb = 2 * (wv_ % 2)
                tw = [(SV, P.cnt[SV]),
                      (SS, xqst_done.get((t_ - 1, wv_), pc_ss0))]
                for j in range(8):
                    kk = wv_ * 8 + j
                    P.op("t", lambda e, kk=kk, j=j, tb=tb, buf=buf: e.matmul(
                        PS[tb + j // 4][:, (j % 4) * 128:(j % 4 + 1) * 128],
                        qbf[:, buf * H + kk * 128:buf * H + (kk + 1) * 128],
                        ident_bf[:, :], start=True, stop=True),
                        SP, 1, waits=tw if j == 0 else ())
                edep = [(SP, P.cnt[SP]),
                        (TS0, xqst_done.get(("st", t_ - 2), 0))]
                for j in range(8):
                    kk = wv_ * 8 + j
                    P.op("a", lambda e, kk=kk, j=j, tb=tb, buf=buf: e.activation(
                        xqtst[:, buf * KT * 128 + kk * 128:
                              buf * KT * 128 + (kk + 1) * 128],
                        PS[tb + j // 4][:, (j % 4) * 128:(j % 4 + 1) * 128],
                        AF.Copy), SS, 1, waits=edep if j == 0 else (), selfw=True)
                xqst_done[(t_, wv_)] = P.cnt[SS]
            trq_done[t_] = P.cnt[SP]
            P.op("s", lambda e, t_=t_, buf=buf, xqT3=xqT3: e.dma_start(
                out=xqT3[:, :, t_ * 128:(t_ + 1) * 128],
                in_=xqtst[:, buf * KT * 128:(buf + 1) * KT * 128].rearrange(
                    "p (k m) -> p k m", k=KT)), TS0, 16,
                waits=[(SS, P.cnt[SS]),
                       (TS0, xqst_done.get(("st", t_ - 1), 0))])
            xqst_done[("st", t_)] = P.cnt[TS0]
            xqst_done[t_] = P.cnt[TS0]
        P.op("v", lambda e: e.tensor_scalar(
            s1_col[:, :], xs_col[:, :], wsb[:, 0:1], None, OP.mult), SV, 1)
        P.op("v", lambda e: e.tensor_scalar(
            s2_col[:, :], xs_col[:, :], wsb[:, 1:2], None, OP.mult), SV, 1)
        sv_s12 = P.cnt[SV]
        P.flush()

    pd_done = P.finals(["sp", "sv", "ss", "s2", "s3", "tl0", "tl1", "ts0"])
    hqT3 = hqT_hbm.rearrange("(k p) t -> p k t", p=128)
    with nc.sbuf_tensor("xqsl", [128, KT * 512], BF) as xqsl, \
         nc.sbuf_tensor("gsl", [128, 2 * KT * 512], BF) as gsl, \
         nc.sbuf_tensor("usl", [128, 2 * KT * 512], BF) as usl, \
         nc.sbuf_tensor("hbuf", [128, 4 * FI], BF) as hbuf, \
         nc.sbuf_tensor("gstage", [128, 2 * 512], F32) as gstage, \
         nc.sbuf_tensor("apart", [128, 4 * 32], F32) as apart, \
         nc.sbuf_tensor("vstage", [128, FI // 8], F32) as vstage, \
         nc.sbuf_tensor("hqtst", [128, 2 * FIT * 128], BF) as hqtst:
        xqT3 = xqT_hbm.rearrange("(k p) t -> p k t", p=128)
        prev_mb = {}
        trp_done = {}
        hqst_done = {}
        hqT_cnt = {}
        for mb in range(4):
            xw = [(XT, P.cnt[XT]), (SP, prev_mb.get(mb - 1, 0))]
            if mb == 0:
                xw += list(pd_done)
            P.op("g", lambda e, mb=mb, xqT3=xqT3: e.dma_start(
                out=xqsl[:, :].rearrange("p (k t) -> p k t", k=KT),
                in_=xqT3[:, :, mb * 512:(mb + 1) * 512]), XT, 16, waits=xw)
            xdep = (XT, P.cnt[XT])
            evs = {}
            evv = {}
            for g in range(16):
                gg = mb * 16 + g
                r_, io_ = g // 2, (g % 2) * 512
                sl = g % 2
                gw = [(SP, prev_mb.get(("g", gg - 2), 0))]
                if gg == 0:
                    gw.append((CC, AG2_DONE))
                P.op("g", lambda e, r_=r_, io_=io_, sl=sl: e.dma_start(
                    out=gsl[:, sl * KT * 512:(sl + 1) * KT * 512].rearrange(
                        "p (k t) -> p k t", k=KT),
                    in_=wqgT_all[r_, io_ // 512]), LS[sl], 16, waits=gw)
                uw = [(SP, prev_mb.get(("g", gg - 2), 0))]
                if gg == 0:
                    uw += list(pd_done) + [(CC, AG2_DONE), (TS0, P.cnt[TS0])]
                P.op("s", lambda e, r_=r_, io_=io_, sl=sl: e.dma_start(
                    out=usl[:, sl * KT * 512:(sl + 1) * KT * 512].rearrange(
                        "p (k t) -> p k t", k=KT),
                    in_=wquT_all[r_, io_ // 512]), US[sl], 16, waits=uw)
                gdep = (LS[sl], P.cnt[LS[sl]])
                gdep2 = (US[sl], P.cnt[US[sl]])
                for m in range(4):
                    bg, bu = (m % 2) * 2, (m % 2) * 2 + 1
                    pk = (g, m - 2) if m >= 2 else (g - 1, m + 2)
                    wv = [xdep, gdep, gdep2, (SV, evv.get(pk, 0)),
                          (SS, evs.get(pk, 0))]
                    for k in range(KT):
                        lhs = xqsl[:, k * 512 + m * 128:k * 512 + (m + 1) * 128]
                        P.op("t", lambda e, k=k, bg=bg, lhs=lhs, sl=sl: e.matmul(
                            PS[bg][:, :], lhs,
                            gsl[:, sl * KT * 512 + k * 512:sl * KT * 512 + (k + 1) * 512],
                            start=(k == 0), stop=(k == KT - 1)),
                            SP, 1, waits=wv if k == 0 else ())
                        P.op("t", lambda e, k=k, bu=bu, lhs=lhs, sl=sl: e.matmul(
                            PS[bu][:, :], lhs,
                            usl[:, sl * KT * 512 + k * 512:sl * KT * 512 + (k + 1) * 512],
                            start=(k == 0), stop=(k == KT - 1)), SP, 1)
                    prev_mb[("g", gg)] = P.cnt[SP]
                    mmdep = [(SP, P.cnt[SP])]
                    t_glob = mb * 4 + m
                    gb = (m % 2) * 512
                    sw2 = mmdep + ([(SV, sv_s12)] if gg == 0 else [])
                    P.op("a", lambda e, bg=bg, gb=gb, t_glob=t_glob: e.activation(
                        gstage[:, gb:gb + 512], PS[bg][:, :], AF.Silu,
                        bias=0.0, scale=s1_col[:, t_glob:t_glob + 1]),
                        SS, 1, waits=sw2, selfw=True)
                    evs[(g, m)] = P.cnt[SS]
                    hw = [(SS, P.cnt[SS]), (SP, trp_done.get((mb - 1, m), 0))]
                    P.op("v", lambda e, bu=bu, gb=gb, m=m, g=g, t_glob=t_glob:
                         e.scalar_tensor_tensor(
                             hbuf[:, m * FI + g * 512:m * FI + (g + 1) * 512],
                             PS[bu][:, :], s2_col[:, t_glob:t_glob + 1],
                             gstage[:, gb:gb + 512], OP.mult, OP.mult),
                         SV, 1, waits=hw)
                    P.op("v", lambda e, m=m, g=g: e.tensor_scalar(
                        vstage[:, 0:512],
                        hbuf[:, m * FI + g * 512:m * FI + (g + 1) * 512],
                        1.0, None, OP.mult, OP.max,
                        accum_out=apart[:, m * 32 + 2 * g:m * 32 + 2 * g + 1]),
                        SV, 1, selfw=True)
                    P.op("v", lambda e, m=m, g=g: e.tensor_scalar(
                        vstage[:, 0:512],
                        hbuf[:, m * FI + g * 512:m * FI + (g + 1) * 512],
                        -1.0, None, OP.mult, OP.max,
                        accum_out=apart[:, m * 32 + 2 * g + 1:m * 32 + 2 * g + 2]),
                        SV, 1, selfw=True)
                    evv[(g, m)] = P.cnt[SV]
            for m in range(4):
                t_glob = mb * 4 + m
                spar = S2 if m % 2 == 0 else S3
                P.op("v", lambda e, m=m, t_glob=t_glob: e.reduce_max(
                    hs_col[:, t_glob:t_glob + 1], apart[:, m * 32:(m + 1) * 32],
                    AX.X), SV, 1)
                P.op("v", lambda e, t_glob=t_glob: e.tensor_scalar(
                    hs_col[:, t_glob:t_glob + 1], hs_col[:, t_glob:t_glob + 1],
                    float(np.float32(1.0) / np.float32(127.0)), EPS,
                    OP.mult, OP.add), SV, 1)
                P.op("v", lambda e, t_glob=t_glob: e.reciprocal(
                    rhs_col[:, t_glob:t_glob + 1], hs_col[:, t_glob:t_glob + 1]),
                    SV, 1)
                for hf in range(8):
                    P.op("v", lambda e, m=m, hf=hf, t_glob=t_glob: e.tensor_scalar(
                        vstage[:, :],
                        hbuf[:, m * FI + hf * (FI // 8):m * FI + (hf + 1) * (FI // 8)],
                        rhs_col[:, t_glob:t_glob + 1], RND, OP.mult, OP.add),
                        SV, 1, selfw=True)
                    P.op("v", lambda e, m=m, hf=hf: e.tensor_single_scalar(
                        hbuf[:, m * FI + hf * (FI // 8):m * FI + (hf + 1) * (FI // 8)],
                        vstage[:, :], -RND, OP.add), SV, 1, selfw=True)
                vq = P.cnt[SV]
                hb = (t_glob % 2) * FIT * 128
                for wv_ in range(8):
                    tb = 4 + 2 * (wv_ % 2)
                    tw = [(SV, vq),
                          (SS, hqst_done.get((t_glob, wv_ - 2),
                                             hqst_done.get((t_glob - 1, wv_ + 6), 0)))]
                    for j in range(8):
                        kk = wv_ * 8 + j
                        P.op("t", lambda e, kk=kk, j=j, tb=tb, m=m: e.matmul(
                            PS[tb + j // 4][:, (j % 4) * 128:(j % 4 + 1) * 128],
                            hbuf[:, m * FI + kk * 128:m * FI + (kk + 1) * 128],
                            ident_bf[:, :], start=True, stop=True),
                            SP, 1, waits=tw if j == 0 else ())
                    edep = [(SP, P.cnt[SP]),
                            (spar, hqT_cnt.get(t_glob - 2, 0))]
                    for j in range(8):
                        kk = wv_ * 8 + j
                        P.op("a", lambda e, kk=kk, j=j, tb=tb, hb=hb: e.activation(
                            hqtst[:, hb + kk * 128:hb + (kk + 1) * 128],
                            PS[tb + j // 4][:, (j % 4) * 128:(j % 4 + 1) * 128],
                            AF.Copy), SS, 1, waits=edep if j == 0 else (),
                            selfw=True)
                    hqst_done[(t_glob, wv_)] = P.cnt[SS]
                trp_done[(mb, m)] = P.cnt[SP]
                P.op("g", lambda e, t_glob=t_glob, hb=hb: e.dma_start(
                    out=hqT3[:, :, t_glob * 128:(t_glob + 1) * 128],
                    in_=hqtst[:, hb:hb + FIT * 128].rearrange(
                        "p (k m) -> p k m", k=FIT)),
                    spar, 16, waits=[(SS, P.cnt[SS])])
                hqT_cnt[t_glob] = P.cnt[spar]
            prev_mb[mb] = P.cnt[SP]
        P.op("v", lambda e: e.tensor_scalar(
            s3_col[:, :], hs_col[:, :], wsb[:, 2:3], None, OP.mult), SV, 1)
        sv_s3 = P.cnt[SV]
        P.flush()

    pe1_done = P.finals(["sp", "sv", "ss", "s2", "s3", "xt",
                         "ls0", "ls1", "ls2", "ls3", "us0", "us1"])
    res2_done = {}
    with nc.sbuf_tensor("hqf", [128, FIT * HT], BF) as hqf, \
         nc.sbuf_tensor("wdsl", [128, 2 * FIT * 128], BF) as wdsl, \
         nc.sbuf_tensor("tstage2", [128, HT], F32) as tstage2, \
         nc.sbuf_tensor("ystrip", [128, 2 * 8 * 128], F32) as ystrip, \
         nc.sbuf_tensor("rstage2", [128, 2 * 8 * 128], F32) as rstage2:
        y3j = y1_hbm.rearrange("(j p) h -> p j h", p=128)
        r23 = res2_hbm.rearrange("(j p) h -> p j h", p=128)
        wdT3 = wqdT_all.rearrange("r M p k m -> M p r k m")
        mm_done = {}
        ev_done = {}
        tr_done = {}
        pend_tr = None
        for half in range(2):
            hw = [(XT, P.cnt[XT]), (SP, mm_done.get(half - 1, 0))]
            if half == 0:
                hw += list(pe1_done)
            P.op("g", lambda e, half=half, hqT3=hqT3: e.dma_start(
                out=hqf[:, :].rearrange("p (k t) -> p k t", k=FIT),
                in_=hqT3[:, :, half * HT:(half + 1) * HT]), XT, 16, waits=hw)
            hdep = (XT, P.cnt[XT])
            for M in range(KT):
                sl = M % 2
                FH = FIT // 2
                d1 = wdsl[:, sl * FIT * 128:sl * FIT * 128 + FH * 128].rearrange(
                    "p (r k m) -> p r k m", r=4, k=8)
                d2 = wdsl[:, sl * FIT * 128 + FH * 128:(sl + 1) * FIT * 128
                          ].rearrange("p (r k m) -> p r k m", r=4, k=8)
                wmw = [(SP, mm_done.get((half, M - 2), mm_done.get(half - 1, 0)))]
                P.op("g", lambda e, d1=d1, M=M, wdT3=wdT3: e.dma_start(
                    out=d1, in_=wdT3[M][:, 0:4, :, :]),
                    LS[sl], 16, waits=wmw)
                wmw2 = list(wmw)
                if half == 0 and M == 0:
                    wmw2 += list(pe1_done) + [(TS0, P.cnt[TS0])]
                P.op("s", lambda e, d2=d2, M=M, wdT3=wdT3: e.dma_start(
                    out=d2, in_=wdT3[M][:, 4:8, :, :]),
                    WD[sl], 16, waits=wmw2)
                ysw = [(SV, ev_done.get((half, M - 2),
                                        ev_done.get((half - 1, KT - 2 + M % 2), 0)))]
                if M == 0:
                    ysw += [y1_st[half * 8 + 6], y1_st[half * 8 + 7]]
                    if half == 0:
                        ysw += list(pe1_done)
                P.op("s", lambda e, M=M, half=half, sl=sl, y3j=y3j: e.dma_start(
                    out=ystrip[:, sl * 1024:(sl + 1) * 1024].rearrange(
                        "p (j m) -> p j m", j=8),
                    in_=y3j[:, half * 8:(half + 1) * 8, M * 128:(M + 1) * 128]),
                    XS[sl], 16, waits=ysw)
                wdep = (LS[sl], P.cnt[LS[sl]])
                wdep2 = (WD[sl], P.cnt[WD[sl]])
                bs = 2 * (M % 2)
                wv = [hdep, wdep, wdep2, (SS, ev_done.get(("s", half, M - 2), 0))]
                for k in range(FIT):
                    lhs = wdsl[:, sl * FIT * 128 + k * 128:sl * FIT * 128 + (k + 1) * 128]
                    for n in range(2):
                        P.op("t", lambda e, k=k, n=n, bs=bs, lhs=lhs: e.matmul(
                            PS[bs + n][:, :], lhs,
                            hqf[:, k * HT + n * 512:k * HT + (n + 1) * 512],
                            start=(k == 0), stop=(k == FIT - 1)),
                            SP, 1, waits=wv if (k == 0 and n == 0) else ())
                mm_done[(half, M)] = P.cnt[SP]
                mm_done[half] = P.cnt[SP]
                if pend_tr is not None:
                    pend_tr()
                mmdep = [(SP, P.cnt[SP])]
                for n in range(2):
                    P.op("a", lambda e, n=n, bs=bs: e.activation(
                        tstage2[:, n * 512:(n + 1) * 512], PS[bs + n][:, :], AF.Copy),
                        SS, 1, waits=mmdep if n == 0 else (), selfw=True)
                ev_done[("s", half, M)] = P.cnt[SS]
                sdep = [(SS, P.cnt[SS]),
                        (SV, tr_done.get((half, M - 2),
                                         tr_done.get((half - 1, KT - 2 + M % 2), 0)))]
                tbs = 4 + 2 * (M % 2)

                def emit_tr2(M=M, half=half, sl=sl, sdep=tuple(sdep), tbs=tbs):
                    for j in range(8):
                        P.op("t", lambda e, j=j, tbs=tbs: e.transpose(
                            PS[tbs + j // 4][:, (j % 4) * 128:(j % 4 + 1) * 128],
                            tstage2[:, j * 128:(j + 1) * 128], ident[:, :]),
                            SP, 1, waits=sdep if j == 0 else ())
                    spar = S2 if sl == 0 else S3
                    trdep = [(SP, P.cnt[SP]), (XS[sl], P.cnt[XS[sl]]),
                             (spar, P.cnt[spar])] + \
                            ([(SV, sv_s3)] if (half == 0 and M == 0) else [])
                    for j in range(8):
                        t_idx = half * 8 + j
                        P.op("v", lambda e, j=j, sl=sl, tbs=tbs, t_idx=t_idx:
                             e.scalar_tensor_tensor(
                                 rstage2[:, sl * 1024 + j * 128:sl * 1024 + (j + 1) * 128],
                                 PS[tbs + j // 4][:, (j % 4) * 128:(j % 4 + 1) * 128],
                                 s3_col[:, t_idx:t_idx + 1],
                                 ystrip[:, sl * 1024 + j * 128:sl * 1024 + (j + 1) * 128],
                                 OP.mult, OP.add),
                             SV, 1, waits=trdep if j == 0 else ())
                    tr_done[(half, M)] = P.cnt[SV]
                    ev_done[(half, M)] = P.cnt[SV]
                    P.op("g", lambda e, M=M, half=half, sl=sl: e.dma_start(
                        out=r23[:, half * 8:(half + 1) * 8, M * 128:(M + 1) * 128],
                        in_=rstage2[:, sl * 1024:(sl + 1) * 1024].rearrange(
                            "p (j m) -> p j m", j=8)), spar, 16,
                        waits=[(SV, P.cnt[SV])])
                    res2_done[half] = (spar, P.cnt[spar])
                pend_tr = emit_tr2
            pend_tr()
            pend_tr = None
        P.flush()

    pe2_done = P.finals(["sp", "sv", "ss", "s2", "s3", "xt", "ls0", "ls1",
                         "wd0", "wd1", "xs0", "xs1"])
    with nc.sbuf_tensor("f_t", [128, 2 * H], F32) as f_t, \
         nc.sbuf_tensor("f_s", [128, 2 * H], F32) as f_s:
        r2t = res2_hbm.rearrange("(t p) h -> t p h", p=128)
        o3 = out_t.rearrange("(t p) h -> t p h", p=128)
        sq_done = {}
        o_st = {}
        o_spar = {}
        for t_ in range(TT):
            buf = t_ % 2
            spar = S2 if buf == 0 else S3
            lw = [(TL[buf], P.cnt[TL[buf]]), (SS, sq_done.get(t_ - 2, 0)),
                  (SV, o_st.get(t_ - 2, 0))]
            if t_ < 2:
                lw += list(pe2_done)
            P.op("g", lambda e, t_=t_, buf=buf, r2t=r2t: e.dma_start(
                out=f_t[:, buf * H:(buf + 1) * H], in_=r2t[t_]), TL[buf], 16,
                waits=lw)
            dep = [(TL[buf], P.cnt[TL[buf]])]
            P.op("v", lambda e, t_=t_, buf=buf: e.reduce_sum(
                mean_col[:, t_:t_ + 1], f_t[:, buf * H:(buf + 1) * H], AX.X),
                SV, 1, waits=dep)
            P.op("v", lambda e, t_=t_: e.tensor_single_scalar(
                mean_col[:, t_:t_ + 1], mean_col[:, t_:t_ + 1], 1.0 / H, OP.mult),
                SV, 1)
            P.op("v", lambda e, t_=t_: e.tensor_single_scalar(
                nmean_col[:, t_:t_ + 1], mean_col[:, t_:t_ + 1], -1.0, OP.mult),
                SV, 1)
            sqw = [(SV, P.cnt[SV])] + ([o_spar[t_ - 2]] if t_ >= 2 else [])
            P.op("a", lambda e, t_=t_, buf=buf: e.activation(
                f_s[:, buf * H:(buf + 1) * H], f_t[:, buf * H:(buf + 1) * H],
                AF.Square, bias=nmean_col[:, t_:t_ + 1], scale=1.0,
                accum_out=var_col[:, t_:t_ + 1]),
                SS, 1, waits=sqw, selfw=True)
            P.op("v", lambda e, t_=t_: e.tensor_scalar(
                var_col[:, t_:t_ + 1], var_col[:, t_:t_ + 1], 1.0 / H, EPS,
                OP.mult, OP.add), SV, 1, waits=[(SS, P.cnt[SS])])
            P.op("v", lambda e, t_=t_: e.reciprocal(
                rstd_col[:, t_:t_ + 1], var_col[:, t_:t_ + 1]), SV, 1)
            P.op("a", lambda e, t_=t_: e.activation(
                rstd_col[:, t_:t_ + 1], rstd_col[:, t_:t_ + 1], AF.Sqrt),
                SS, 1, waits=[(SV, P.cnt[SV])], selfw=True)
            sq_done[t_] = P.cnt[SS]
            P.op("v", lambda e, t_=t_: e.tensor_mul(
                nmean_col[:, t_:t_ + 1], rstd_col[:, t_:t_ + 1],
                rstd_col[:, t_:t_ + 1]), SV, 1, waits=[(SS, P.cnt[SS])])
            P.op("v", lambda e, t_=t_: e.tensor_mul(
                nmean_col[:, t_:t_ + 1], nmean_col[:, t_:t_ + 1],
                var_col[:, t_:t_ + 1]), SV, 1)
            P.op("v", lambda e, t_=t_: e.tensor_scalar(
                nmean_col[:, t_:t_ + 1], nmean_col[:, t_:t_ + 1], -0.5, 1.5,
                OP.mult, OP.add), SV, 1)
            P.op("v", lambda e, t_=t_: e.tensor_mul(
                rstd_col[:, t_:t_ + 1], rstd_col[:, t_:t_ + 1],
                nmean_col[:, t_:t_ + 1]), SV, 1)
            ow = [(SV, o_st[t_ - 2])] if t_ >= 2 else []
            P.op("v", lambda e, t_=t_, buf=buf: e.tensor_scalar(
                f_s[:, buf * H:(buf + 1) * H], f_t[:, buf * H:(buf + 1) * H],
                mean_col[:, t_:t_ + 1], rstd_col[:, t_:t_ + 1],
                OP.subtract, OP.mult), SV, 1, waits=ow, selfw=True)
            o_st[t_] = P.cnt[SV]
            P.op("g", lambda e, t_=t_, buf=buf, o3=o3: e.dma_start(
                out=o3[t_], in_=f_s[:, buf * H:(buf + 1) * H]), spar, 16,
                waits=[(SV, P.cnt[SV])])
            o_spar[t_] = (spar, P.cnt[spar])
        P.barrier()
        P.flush()

    ctx.__exit__(None, None, None)
    return nc


_CACHE = {}


def _get_nc():
    if "nc" not in _CACHE:
        _CACHE["nc"] = build_nc()
    return _CACHE["nc"]


def kernel(**inputs):
    x = np.asarray(inputs["L0_attn_x"], dtype=np.float32)
    init = np.ascontiguousarray(np.asarray(inputs["L0_attn_init_state"],
                                           dtype=np.float32))
    aw = {w: np.ascontiguousarray(
        np.asarray(inputs["L0_attn_w_" + w], dtype=np.float32)
        .reshape(KT, 128, KT, 128).transpose(2, 1, 0, 3))
        for w in ["f", "i", "v", "q", "ro", "up", "gate", "down"]}
    fg = np.asarray(inputs["L0_ffn_w_gate"], dtype=np.float32)
    fu = np.asarray(inputs["L0_ffn_w_up"], dtype=np.float32)
    fd = np.asarray(inputs["L0_ffn_w_down"], dtype=np.float32)
    ident = np.eye(128, dtype=np.float32)

    nc = _get_nc()
    in_maps = []
    for c in range(CORES):
        b, q = c // 4, c % 4
        m = np.zeros((128, 27), np.float32)
        for kk in (1, 2, 3):
            if q >= kk:
                m[:, (kk - 1) * 8 + (c - kk)] = 1.0
            else:
                m[:, 23 + kk] = 1.0
        im = {
            "x_c": np.ascontiguousarray(x[b, q * T:(q + 1) * T, :]),
            "init_state": init,
            "ffn_gate_sl": np.ascontiguousarray(fg[c * ISL:(c + 1) * ISL, :]),
            "ffn_up_sl": np.ascontiguousarray(fu[c * ISL:(c + 1) * ISL, :]),
            "ffn_down_sl": np.ascontiguousarray(fd[:, c * ISL:(c + 1) * ISL]),
            "ident": ident,
            "scan_masks": m,
        }
        for w, arr in aw.items():
            im["w_" + w] = arr
        in_maps.append(im)

    res = run_bass_kernel_spmd(nc, in_maps, core_ids=list(range(CORES)))
    out = np.stack([np.asarray(res.results[c]["out_c"]) for c in range(CORES)])
    return out.reshape(2, 4, T, H).reshape(2, 8192, H)
